# version 11
# baseline (speedup 1.0000x reference)
import math
from contextlib import ExitStack

import numpy as np

import concourse.bass as bass
import concourse.mybir as mybir
from concourse.bass_utils import run_bass_kernel_spmd

F32 = mybir.dt.float32
BF16 = mybir.dt.bfloat16
I8 = mybir.dt.int8
AF = mybir.ActivationFunctionType
ALU = mybir.AluOpType
AX = mybir.AxisListType

B, S, DM = 4, 4096, 4096
H, G, HD = 32, 8, 128
NIC = DM // 128
NT = S // 128
EPS = 1e-5
NEGBIG = -30000.0
QSCALE = 1.0 / math.sqrt(128.0)

ENGS = ("sync", "gpsimd", "tensor", "scalar", "vector")


def own_tiles(p):
    return [2 * r + p for r in range(8)] + [31 - 2 * t - p for t in range(8)]


def nkb_of(r):
    return 2 * r + 2 if r < 8 else 32 - 2 * (r - 8)


GROUPS = [[0, 1, 2, 3], [4, 5, 6, 7], [8, 9, 10, 11], [12, 13, 14, 15]]
NKB_GRP = [max(nkb_of(r) for r in grp) for grp in GROUPS]


class Prog:

    def __init__(self, nc):
        self.nc = nc
        self.ev = {e: [] for e in ENGS}
        self.cnt = {}

    def at(self, name):
        return (name, self.cnt.get(name, 0))

    def on(self, eng, fn, waits=(), inc=None, n=1):
        w = [x for x in waits if x is not None and x[1] > 0]
        tok = None
        if inc is not None:
            self.cnt[inc] = self.cnt.get(inc, 0) + n
            tok = (inc, self.cnt[inc])
        self.ev[eng].append((fn, w, inc, n))
        return tok

    def dma(self, eng, out, in_, waits=(), inc=None):
        if inc is None:
            inc = "gp" if eng == "gpsimd" else "sp"
        return self.on(
            eng, (lambda o, i: lambda e: e.dma_start(out=o, in_=i))(out, in_),
            waits=waits, inc=inc, n=16)

    def build(self, block, ctx):
        handles = {name: ctx.enter_context(self.nc.semaphore(name))
                   for name in self.cnt}

        def mk(eng_name):
            evs = self.ev[eng_name]

            def body(e):
                waited = {}
                for fn, waits, inc, n in evs:
                    for s, v in waits:
                        if waited.get(s, -1) < v:
                            e.wait_ge(handles[s], v)
                            waited[s] = v
                    ins = fn(e)
                    if inc is not None:
                        ins.then_inc(handles[inc], n)
            return body

        for name in ENGS:
            getattr(block, name)(mk(name))


def build_nc():
    nc = bass.Bass("TRN2", target_bir_lowering=False)

    dt = nc.dram_tensor
    x_full = dt("x_full", [S, DM], F32, kind="ExternalInput").ap()
    x_own = dt("x_own", [16, 128, DM], F32, kind="ExternalInput").ap()
    wq = dt("wq", [DM, DM], F32, kind="ExternalInput").ap()
    wk = dt("wk", [DM, 1024], F32, kind="ExternalInput").ap()
    wv = dt("wv", [DM, 1024], F32, kind="ExternalInput").ap()
    wo = dt("wo", [DM, DM], F32, kind="ExternalInput").ap()
    bq_t = dt("bq_t", [128, 32], F32, kind="ExternalInput").ap()
    bk_t = dt("bk_t", [128, 8], F32, kind="ExternalInput").ap()
    bv_t = dt("bv_t", [128, 8], F32, kind="ExternalInput").ap()
    bo_row = dt("bo_row", [1, DM], F32, kind="ExternalInput").ap()
    bases_t = dt("bases_t", [4, DM, DM], I8, kind="ExternalInput").ap()
    alpha_row = dt("alpha_row", [1, 4 * DM], F32, kind="ExternalInput").ap()
    ccg = dt("ccg", [128, S], F32, kind="ExternalInput").ap()
    ssg = dt("ssg", [128, S], F32, kind="ExternalInput").ap()
    cco = dt("cco", [128, 2048], F32, kind="ExternalInput").ap()
    sso = dt("sso", [128, 2048], F32, kind="ExternalInput").ap()
    m_mask = dt("m_mask", [16, 128, 256], F32, kind="ExternalInput").ap()
    ones_r = dt("ones_r", [1, 128], F32, kind="ExternalInput").ap()
    ident = dt("ident", [128, 128], F32, kind="ExternalInput").ap()
    swapm = dt("swapm", [128, 128], F32, kind="ExternalInput").ap()

    y_out = dt("y", [16, 128, DM], F32, kind="ExternalOutput").ap()

    k_sc = dt("k_sc", [G, 128, S], BF16, kind="Internal").ap()
    v_sc = dt("v_sc", [G, 128, S], BF16, kind="Internal").ap()
    q_sc = dt("q_sc", [H, 128, 2048], BF16, kind="Internal").ap()
    wt_sc = dt("wt_sc", [NIC, 128, DM], BF16, kind="Internal").ap()
    x1_sc = dt("x1_sc", [4, 128, DM], F32, kind="Internal").ap()
    echo_src = dt("echo_sc", [128, 2], F32, kind="Internal").ap()

    p = Prog(nc)

    warm = {"buf": None}

    def phase_barrier():
        toks = [p.at(s) for s in ("sp", "gp", "pe", "act", "dve")]
        for eng in ENGS:
            p.on(eng, lambda e: None, waits=list(toks))
        if warm["buf"] is not None:
            wb = warm["buf"]
            tw = p.on("scalar", lambda e: e.activation(wb[:], wb[:], AF.Exp, scale=0.0),
                      inc="act")
            p.on("scalar", lambda e: e.activation(wb[:], wb[:], AF.Exp, scale=0.0),
                 waits=[tw], inc="act")

    with ExitStack() as ctx:
        E = ctx.enter_context
        sb = lambda name, shape, dty: E(nc.sbuf_tensor(name, shape, dty))

        c_ident = sb("c_ident", [128, 128], BF16)
        c_swap = sb("c_swap", [128, 128], F32)
        c_ones = sb("c_ones", [1, 128], BF16)
        c_bq = sb("c_bq", [128, 32], F32)
        c_bqs = sb("c_bqs", [128, 32], F32)
        c_bk = sb("c_bk", [128, 8], F32)
        c_bv = sb("c_bv", [128, 8], F32)
        c_bo = sb("c_bo", [1, DM], BF16)

        def _load_act_table(e):
            from concourse.hw_specs import get_activation_tables
            tabs = get_activation_tables(nc.m.arch)
            idx = list(tabs.keys()).index("natural_log_exp_and_others")
            return e.add_instruction(mybir.InstLoadActFuncSet(
                name=e.bass.get_next_instruction_name(),
                act_func_set_id=idx, ins=[], outs=[]))
        p.on("scalar", _load_act_table, inc="act")

        t_id = p.dma("gpsimd", c_ident[:], ident)
        p.dma("sync", c_swap[:], swapm)
        t_on = p.dma("gpsimd", c_ones[:], ones_r)
        p.dma("sync", c_bq[:], bq_t)
        p.dma("sync", c_bk[:], bk_t)
        p.dma("sync", c_bv[:], bv_t)
        t_bo = p.dma("gpsimd", c_bo[:], bo_row)
        t_cs0 = p.at("sp")
        p.on("vector", lambda e: e.tensor_scalar_mul(c_bqs[:], c_bq[:], QSCALE),
             waits=[t_cs0], inc="dve")
        c_dmy = sb("c_dmy", [128, 1], F32)
        warm["buf"] = c_dmy
        c_echo = sb("c_echo", [128, 2], F32)
        c_echo_bf = sb("c_echo_bf", [128, 2], BF16)

        def dma_echo(queue):
            if queue == "gpsimd":
                return p.dma(queue, c_echo_bf[:], echo_src)
            return p.dma(queue, c_echo[:], echo_src)
        td1 = p.on("scalar", lambda e: e.activation(c_dmy[:], c_bq[:, 0:1], AF.Exp, scale=0.0),
                   waits=[t_cs0], inc="act")
        p.on("scalar", lambda e: e.activation(c_dmy[:], c_dmy[:], AF.Ln, bias=1.0, scale=0.0),
             waits=[td1], inc="act")

        with ExitStack() as c0:
            s0 = lambda name, shape, dty: c0.enter_context(nc.sbuf_tensor(name, shape, dty))
            a_alpha = s0("a_alpha", [1, 4 * DM], BF16)
            a_bc = s0("a_bc", [128, 4 * DM], BF16)
            btb = [s0(f"btb{i}", [128, 4 * DM], BF16) for i in range(2)]
            wtm1 = s0("wtm1", [128, DM], BF16)
            wtm2 = s0("wtm2", [128, DM], BF16)
            wto = [s0(f"wto{i}", [128, DM], BF16) for i in range(2)]
            pb = c0.enter_context(nc.psum_tensor("p0", [128, 512], F32))

            t_al = p.dma("gpsimd", a_alpha[:], alpha_row)
            evt = None
            for j in range(32):
                sl = slice(j * 512, (j + 1) * 512)
                wts = [t_al, t_on] if j == 0 else [evt]
                t_mm = p.on("tensor", (lambda sl: lambda e: e.matmul(
                    pb[:], c_ones[:], a_alpha[:, sl], start=True, stop=True))(sl),
                    waits=wts, inc="pe")
                evt = p.on("scalar", (lambda sl: lambda e: e.copy(a_bc[:, sl], pb[:]))(sl),
                           waits=[t_mm], inc="act")
            t_abc = evt

            dve_done = {}
            sp_tok = {}
            for ic in range(NIC):
                par = ic % 2
                tok = None
                for k in range(4):
                    tok = p.dma("gpsimd", btb[par][:, k * DM:(k + 1) * DM],
                                bases_t[k, ic * 128:(ic + 1) * 128, :],
                                waits=[dve_done.get(ic - 2)] if k == 0 else [])
                bt = btb[par]
                p.on("vector", (lambda bt: lambda e: e.tensor_mul(
                    wtm1[:], bt[:, 0:DM], a_bc[:, 0:DM]))(bt),
                    waits=[tok, t_abc], inc="dve")
                p.on("vector", (lambda bt: lambda e: e.tensor_mul(
                    wtm2[:], bt[:, DM:2 * DM], a_bc[:, DM:2 * DM]))(bt), inc="dve")
                p.on("vector", lambda e: e.tensor_add(wtm1[:], wtm1[:], wtm2[:]), inc="dve")
                p.on("vector", (lambda bt: lambda e: e.tensor_mul(
                    wtm2[:], bt[:, 2 * DM:3 * DM], a_bc[:, 2 * DM:3 * DM]))(bt), inc="dve")
                p.on("vector", lambda e: e.tensor_add(wtm1[:], wtm1[:], wtm2[:]), inc="dve")
                p.on("vector", (lambda bt: lambda e: e.tensor_mul(
                    wtm2[:], bt[:, 3 * DM:4 * DM], a_bc[:, 3 * DM:4 * DM]))(bt), inc="dve")
                wob = wto[par]
                t_w = p.on("vector", (lambda wob: lambda e: e.tensor_add(
                    wob[:], wtm1[:], wtm2[:]))(wob),
                    waits=[sp_tok.get(ic - 2)], inc="dve")
                dve_done[ic] = t_w
                sp_tok[ic] = p.dma("sync", wt_sc[ic], wob[:], waits=[t_w])
        t_p0_done = p.at("sp")
        phase_barrier()

        def mkstats(pre, salloc):
            st = {n: salloc(pre + n, [128, 1], F32)
                  for n in ("sum", "sq", "m", "msq", "m2", "var", "std", "rstd", "nmr")}
            st["_t3"] = None
            return st

        def emit_ln(xbuf, xsq, lnb, st, wait_x, extra_tc=()):
            t1 = p.on("vector", (lambda xbuf, st: lambda e: e.reduce_sum(
                st["sum"][:], xbuf[:], axis=AX.X))(xbuf, st), waits=[wait_x], inc="dve")
            t2 = p.on("scalar", (lambda xbuf, xsq: lambda e: e.activation(
                xsq[:], xbuf[:], AF.Square))(xbuf, xsq),
                waits=[wait_x, st["_t3"]], inc="act")
            t3 = p.on("vector", (lambda xsq, st: lambda e: e.reduce_sum(
                st["sq"][:], xsq[:], axis=AX.X))(xsq, st), waits=[t2], inc="dve")
            st["_t3"] = t3
            t4 = p.on("vector", (lambda st: lambda e: e.tensor_scalar_mul(
                st["m"][:], st["sum"][:], 1.0 / DM))(st), waits=[t1], inc="dve")
            t5 = p.on("vector", (lambda st: lambda e: e.tensor_scalar_mul(
                st["msq"][:], st["sq"][:], 1.0 / DM))(st), waits=[t3], inc="dve")
            t6 = p.on("vector", (lambda st: lambda e: e.tensor_mul(
                st["m2"][:], st["m"][:], st["m"][:]))(st), waits=[t4], inc="dve")
            t7 = p.on("vector", (lambda st: lambda e: e.tensor_sub(
                st["var"][:], st["msq"][:], st["m2"][:]))(st), waits=[t5, t6], inc="dve")
            t7b = p.on("vector", (lambda st: lambda e: e.tensor_scalar_add(
                st["var"][:], st["var"][:], float(EPS)))(st), waits=[t7], inc="dve")
            t8 = p.on("scalar", (lambda st: lambda e: e.activation(
                st["std"][:], st["var"][:], AF.Ln))(st),
                waits=[t7b], inc="act")
            t9 = p.on("scalar", (lambda st: lambda e: e.activation(
                st["rstd"][:], st["std"][:], AF.Exp, scale=-0.5))(st),
                waits=[t8], inc="act")
            ta = p.on("vector", (lambda st: lambda e: e.tensor_mul(
                st["nmr"][:], st["m"][:], st["rstd"][:]))(st), waits=[t9], inc="dve")
            tb = p.on("vector", (lambda st: lambda e: e.tensor_scalar_mul(
                st["nmr"][:], st["nmr"][:], -1.0))(st), waits=[ta], inc="dve")
            tc = p.on("vector", (lambda lnb, xbuf, st: lambda e: e.tensor_scalar(
                lnb[:], xbuf[:], st["rstd"][:], st["nmr"][:],
                op0=ALU.mult, op1=ALU.add))(lnb, xbuf, st),
                waits=[tb] + list(extra_tc), inc="dve")
            return tc

        def emit_rope(pre, pswsl, ccs, sss, outb, t_pre, t_war_psw, t_war_out):
            t_sw = p.on("tensor", (lambda pre, pswsl: lambda e: e.matmul(
                pswsl, c_swap[:], pre[:], start=True, stop=True))(pre, pswsl),
                waits=[t_pre, t_war_psw], inc="pe")
            t_a = p.on("vector", (lambda pre, ccs: lambda e: e.tensor_mul(
                pre[:], pre[:], ccs))(pre, ccs), waits=[t_sw], inc="dve")
            t_b = p.on("vector", (lambda pswsl, sss, outb: lambda e: e.tensor_mul(
                outb[:], pswsl, sss))(pswsl, sss, outb),
                waits=[t_sw, t_war_out], inc="dve")
            t_c = p.on("vector", (lambda pre, outb: lambda e: e.tensor_add(
                outb[:], outb[:], pre[:]))(pre, outb), waits=[t_a, t_b], inc="dve")
            return t_c

        def mk_ln_bufs(tag, salloc):
            return dict(
                xb=[salloc(f"xb{tag}{i}", [128, DM], F32) for i in range(2)],
                xsq=[salloc(f"xsq{tag}{i}", [128, DM], BF16) for i in range(2)],
                lnb=[salloc(f"lnb{tag}{i}", [128, DM], BF16) for i in range(2)],
                stats=[mkstats(f"st{tag}{i}", salloc) for i in range(2)],
                ln_done={}, tr_done={}, base=0)

        def emit_ln_transpose_phase(bufs, nt_tiles, x_src_fn, ln_T, psum_name):
            lnT_v = ln_T[:].rearrange("p (c t) -> p c t", c=NIC)
            ln_done = bufs["ln_done"]
            tr_done = bufs["tr_done"]
            base = bufs["base"]
            with ExitStack() as cq:
                ptr = cq.enter_context(nc.psum_tensor(psum_name, [128, 2048], BF16))
                for j in range(nt_tiles):
                    jj = base + j
                    par = jj % 2
                    p.dma("sync", bufs["xb"][par][:], x_src_fn(j),
                          waits=[ln_done.get(jj - 2)])
                    t_x = dma_echo("sync")
                    t_ln = emit_ln(bufs["xb"][par], bufs["xsq"][par], bufs["lnb"][par],
                                   bufs["stats"][par], t_x,
                                   extra_tc=[tr_done.get((jj - 2, 3))])
                    ln_done[jj] = t_ln
                    for bb in range(4):
                        tpar = (jj * 4 + bb) % 2
                        reg = ptr[:, tpar * 1024:(tpar + 1) * 1024]
                        war = tr_done.get((jj, bb - 2)) if bb >= 2 else tr_done.get((jj - 1, bb + 2))
                        t_tr = None
                        for u in range(8):
                            icx = bb * 8 + u
                            t_tr = p.on("tensor", (lambda reg, u, icx, par: lambda e: e.transpose(
                                reg[:, u * 128:(u + 1) * 128],
                                bufs["lnb"][par][:, icx * 128:(icx + 1) * 128], c_ident[:]))(reg, u, icx, par),
                                waits=[t_ln, war, t_id] if u == 0 else [], inc="pe" if u == 7 else None)
                        dst = lnT_v[:, bb * 8:(bb + 1) * 8, j * 128:(j + 1) * 128]
                        tr_done[(jj, bb)] = p.on(
                            "scalar", (lambda dst, reg: lambda e: e.copy(
                                dst, reg[:].rearrange("p (c u) -> p c u", c=8)))(dst, reg),
                            waits=[t_tr], inc="act")
            bufs["base"] = base + nt_tiles
            return p.at("act")

        with ExitStack() as cA:
            sA = lambda name, shape, dty: cA.enter_context(nc.sbuf_tensor(name, shape, dty))
            ln_T = sA("ln_TA", [128, NIC * 1024], BF16)
            wpan = [sA(f"wpanA{i}", [128, NIC * 128], BF16) for i in range(2)]
            kpre = [sA(f"kpreA{i}", [128, 512], F32) for i in range(2)]
            krop = [sA(f"kropA{i}", [128, 512], BF16) for i in range(2)]
            vstg = [sA(f"vstgA{i}", [128, 512], BF16) for i in range(2)]
            c_ccg = sA("c_ccg", [128, S], F32)
            c_ssg = sA("c_ssg", [128, S], F32)
            p.dma("sync", c_ccg[:], ccg)
            p.dma("sync", c_ssg[:], ssg)
            lnbufsA = mk_ln_bufs("AKV", sA)
            ev_toks = {}
            rope_war = {}
            pvt_war = {}
            spill_tok = {}

            for quarter in range(4):
                t_lnT = emit_ln_transpose_phase(
                    lnbufsA, 8,
                    (lambda q: lambda j: x_full[(q * 8 + j) * 128:(q * 8 + j + 1) * 128, :])(quarter),
                    ln_T, f"pA_tr{quarter}")
                with ExitStack() as c2:
                    pacc = c2.enter_context(nc.psum_tensor(f"pA_acc{quarter}", [128, 2048], F32))
                    psw = c2.enter_context(nc.psum_tensor(f"pA_sw{quarter}", [128, 1024], F32))
                    pvt = c2.enter_context(nc.psum_tensor(f"pA_vt{quarter}", [128, 1024], BF16))
                    tokbase = quarter * 1024
                    for which, (wsrc, bias_c, sc_dst, do_rope) in enumerate(
                            [(wk, c_bk, k_sc, True), (wv, c_bv, v_sc, False)]):
                        for oc in range(8):
                            idx = quarter * 16 + which * 8 + oc
                            wp = wpan[idx % 2]
                            p.dma(
                                "gpsimd", wp[:].rearrange("p (c o) -> p c o", c=NIC),
                                wsrc[:, oc * 128:(oc + 1) * 128].rearrange(
                                    "(c p) o -> p c o", p=128),
                                waits=[ev_toks.get(idx - 2)])
                            t_wl = dma_echo("gpsimd")
                            accpar = idx % 2
                            acc = pacc[:, accpar * 1024:(accpar + 1) * 1024]
                            t_mm = None
                            for ic in range(NIC):
                                for pan in range(2):
                                    wts = []
                                    if ic == 0 and pan == 0:
                                        wts = [t_wl, t_lnT, ev_toks.get(idx - 2)]
                                    t_mm = p.on("tensor", (lambda acc, pan, wp, ic: lambda e: e.matmul(
                                        acc[:, pan * 512:(pan + 1) * 512],
                                        wp[:, ic * 128:(ic + 1) * 128],
                                        ln_T[:, ic * 1024 + pan * 512: ic * 1024 + (pan + 1) * 512],
                                        start=(ic == 0), stop=(ic == NIC - 1)))(acc, pan, wp, ic),
                                        waits=wts, inc="pe" if ic == NIC - 1 else None)
                            t_acc = t_mm
                            pan_evs = []
                            for pan in range(2):
                                ppar = pan
                                kp, kr = kpre[ppar], krop[ppar]
                                t_ev = p.on("scalar", (lambda kp, acc, pan, bias_c, oc: lambda e: e.activation(
                                    kp[:], acc[:, pan * 512:(pan + 1) * 512], AF.Identity,
                                    bias=bias_c[:, oc:oc + 1], scale=1.0))(kp, acc, pan, bias_c, oc),
                                    waits=[t_acc, rope_war.get((idx - 1, pan)),
                                           spill_tok.get((idx - 1, pan))], inc="act")
                                if do_rope:
                                    cslice = c_ccg[:, tokbase + pan * 512: tokbase + (pan + 1) * 512]
                                    sslice = c_ssg[:, tokbase + pan * 512: tokbase + (pan + 1) * 512]
                                    t_ro = emit_rope(kp, psw[:, ppar * 512:(ppar + 1) * 512],
                                                     cslice, sslice, kr, t_ev,
                                                     rope_war.get((idx - 1, pan)),
                                                     spill_tok.get((idx - 1, pan)))
                                    rope_war[(idx, pan)] = t_ro
                                    spill_tok[(idx, pan)] = p.dma(
                                        "sync",
                                        sc_dst[oc][:, tokbase + pan * 512: tokbase + (pan + 1) * 512],
                                        kr[:], waits=[t_ro])
                                else:
                                    t_cast = p.on("vector", (lambda kr, kp: lambda e: e.tensor_copy(
                                        kr[:], kp[:]))(kr, kp),
                                        waits=[t_ev, spill_tok.get((idx - 1, pan))], inc="dve")
                                    rope_war[(idx, pan)] = t_cast
                                    t_tr = None
                                    for u in range(4):
                                        t_tr = p.on("tensor", (lambda ppar, u, kr: lambda e: e.transpose(
                                            pvt[:, ppar * 512 + u * 128: ppar * 512 + (u + 1) * 128],
                                            kr[:, u * 128:(u + 1) * 128], c_ident[:]))(ppar, u, kr),
                                            waits=[t_cast, pvt_war.get((idx - 1, pan))] if u == 0 else [],
                                            inc="pe" if u == 3 else None)
                                    t_ev2 = p.on("scalar", (lambda ppar: lambda e: e.copy(
                                        vstg[ppar][:], pvt[:, ppar * 512:(ppar + 1) * 512]))(ppar),
                                        waits=[t_tr, spill_tok.get((idx - 1, pan))], inc="act")
                                    pvt_war[(idx, pan)] = t_ev2
                                    spill_tok[(idx, pan)] = p.dma(
                                        "sync",
                                        sc_dst[oc][:, tokbase + pan * 512: tokbase + (pan + 1) * 512],
                                        vstg[ppar][:], waits=[t_ev2])
                                pan_evs.append(spill_tok[(idx, pan)])
                            ev_toks[idx] = pan_evs[-1]
                phase_barrier()
        t_akv_done = p.at("sp")

        with ExitStack() as cQ:
            sQ = lambda name, shape, dty: cQ.enter_context(nc.sbuf_tensor(name, shape, dty))
            ln_T = sQ("ln_TQ", [128, NIC * 1024], BF16)
            wpan = [sQ(f"wpanQ{i}", [128, NIC * 128], BF16) for i in range(2)]
            kpre = [sQ(f"kpreQ{i}", [128, 512], F32) for i in range(2)]
            krop = [sQ(f"kropQ{i}", [128, 512], BF16) for i in range(2)]
            c_cco = sQ("c_cco", [128, 2048], F32)
            c_sso = sQ("c_sso", [128, 2048], F32)
            p.dma("sync", c_cco[:], cco)
            p.dma("sync", c_sso[:], sso)
            lnbufsQ = mk_ln_bufs("AQ", sQ)
            ev_toksQ = {}
            rope_warQ = {}
            spill_tokQ = {}

            for half in range(2):
                t_lnT = emit_ln_transpose_phase(
                    lnbufsQ, 8,
                    (lambda hh: lambda j: x_own[hh * 8 + j])(half),
                    ln_T, f"pQ_tr{half}")
                with ExitStack() as c2:
                    pacc = c2.enter_context(nc.psum_tensor(f"pQ_acc{half}", [128, 2048], F32))
                    psw = c2.enter_context(nc.psum_tensor(f"pQ_sw{half}", [128, 1024], F32))
                    tokbase = half * 1024
                    ev_toks = ev_toksQ
                    rope_war = rope_warQ
                    spill_tok = spill_tokQ
                    for oc0 in range(H):
                        oc = oc0
                        gq = half * 32 + oc0
                        wp = wpan[gq % 2]
                        p.dma(
                            "gpsimd", wp[:].rearrange("p (c o) -> p c o", c=NIC),
                            wq[:, oc * 128:(oc + 1) * 128].rearrange("(c p) o -> p c o", p=128),
                            waits=[ev_toks.get(gq - 2)])
                        t_wl = dma_echo("gpsimd")
                        accpar = gq % 2
                        acc = pacc[:, accpar * 1024:(accpar + 1) * 1024]
                        t_mm = None
                        for ic in range(NIC):
                            for pan in range(2):
                                wts = []
                                if ic == 0 and pan == 0:
                                    wts = [t_wl, t_lnT, ev_toks.get(gq - 2)]
                                t_mm = p.on("tensor", (lambda acc, pan, wp, ic: lambda e: e.matmul(
                                    acc[:, pan * 512:(pan + 1) * 512],
                                    wp[:, ic * 128:(ic + 1) * 128],
                                    ln_T[:, ic * 1024 + pan * 512: ic * 1024 + (pan + 1) * 512],
                                    start=(ic == 0), stop=(ic == NIC - 1)))(acc, pan, wp, ic),
                                    waits=wts, inc="pe" if ic == NIC - 1 else None)
                        t_acc = t_mm
                        last = None
                        for pan in range(2):
                            ppar = pan
                            kp, kr = kpre[ppar], krop[ppar]
                            t_ev = p.on("scalar", (lambda kp, acc, pan, oc: lambda e: e.activation(
                                kp[:], acc[:, pan * 512:(pan + 1) * 512], AF.Identity,
                                bias=c_bqs[:, oc:oc + 1], scale=QSCALE))(kp, acc, pan, oc),
                                waits=[t_acc, rope_war.get((gq - 1, pan)),
                                       spill_tok.get((gq - 1, pan))], inc="act")
                            cslice = c_cco[:, tokbase + pan * 512: tokbase + (pan + 1) * 512]
                            sslice = c_sso[:, tokbase + pan * 512: tokbase + (pan + 1) * 512]
                            t_ro = emit_rope(kp, psw[:, ppar * 512:(ppar + 1) * 512],
                                             cslice, sslice, kr, t_ev,
                                             rope_war.get((gq - 1, pan)),
                                             spill_tok.get((gq - 1, pan)))
                            rope_war[(gq, pan)] = t_ro
                            spill_tok[(gq, pan)] = p.dma(
                                "sync", q_sc[oc][:, tokbase + pan * 512: tokbase + (pan + 1) * 512],
                                kr[:], waits=[t_ro])
                            last = spill_tok[(gq, pan)]
                        ev_toks[gq] = last
                phase_barrier()

        t_aq_done = p.at("sp")

        with ExitStack() as cB:
            sB = lambda name, shape, dty: cB.enter_context(nc.sbuf_tensor(name, shape, dty))
            qg = sB("qg", [128, H * 512], BF16)
            kbuf = [sB(f"kbuf{i}", [128, 4096], BF16) for i in range(2)]
            vbuf = [sB(f"vbuf{i}", [128, 4096], BF16) for i in range(2)]
            sc32 = [sB(f"sc32{i}", [128, 4096], F32) for i in range(2)]
            pbuf = [sB(f"pbuf{i}", [128, 4096], BF16) for i in range(2)]
            ptb = [sB(f"ptb{i}", [128, 1024], BF16) for i in range(2)]
            actT = sB("actT", [128, H * 512], BF16)
            actT_v = actT[:].rearrange("p (c t) -> p c t", c=NIC)
            wpanB = [sB(f"wpanB{i}", [128, NIC * 256], BF16) for i in range(2)]
            attnB = sc32[0]
            xB = sc32[1]
            lnbB = pbuf[0]
            xsqB = pbuf[1]
            statB = mkstats("stB", sB)
            mB = sB("mB", [128, 4 * 256], F32)
            x1sl = [sB(f"x1sl{i}", [128, 512], F32) for i in range(2)]
            yst = [sB(f"yst{i}", [128, 512], F32) for i in range(2)]
            att_st = [sB(f"attst{i}", [128, 512], F32) for i in range(2)]
            negmax = [sB(f"negmax{i}", [128, 1], F32) for i in range(2)]
            ssum = [sB(f"ssum{i}", [128, 1], F32) for i in range(2)]
            rsum = [sB(f"rsum{i}", [128, 1], F32) for i in range(2)]

            t_b5_sp = {}
            t_b5_pe = {}
            for g in range(4):
                grp = GROUPS[g]
                nkbm = NKB_GRP[g]
                phase_barrier()
                t_qload = p.dma("sync", qg[:].rearrange("p (h t) -> p h t", h=H),
                                q_sc[:, :, g * 512:(g + 1) * 512].rearrange("h p t -> p h t"),
                                waits=[t_aq_done, t_b5_sp.get(g - 1), t_b5_pe.get(g - 1)])
                t_m = p.dma("sync", mB[:].rearrange("p (r c) -> p r c", r=4),
                            m_mask[g * 4:(g + 1) * 4].rearrange("r p c -> p r c"),
                            waits=[t_b5_sp.get(g - 1)])

                with ExitStack() as c2:
                    psc = c2.enter_context(nc.psum_tensor(f"pB_sc{g}", [128, 2048], F32))
                    ppt = c2.enter_context(nc.psum_tensor(f"pB_pt{g}", [128, 2048], BF16))
                    pctx = c2.enter_context(nc.psum_tensor(f"pB_ctx{g}", [128, 1024], F32))

                    kv_tok = {0: [
                        p.dma("sync", kbuf[0][:, 0:nkbm * 128], k_sc[0][:, 0:nkbm * 128],
                              waits=[t_akv_done, t_b5_pe.get(g - 1)]),
                        p.dma("sync", vbuf[0][:, 0:nkbm * 128], v_sc[0][:, 0:nkbm * 128],
                              waits=[t_akv_done, t_b5_pe.get(g - 1)])]}
                    t_echo_g = dma_echo("sync")
                    kv_tok[0] = [t_echo_g, t_echo_g]
                    t_qload = t_echo_g
                    t_m = t_echo_g

                    RH = [(gp_, r, h4) for gp_ in range(8) for r in range(4) for h4 in range(4)]
                    st_done = {}
                    sm_done = {}
                    ctx_done = {}
                    ctxmm = {}
                    state = {"creg_i": 0, "ptreg_i": 0}
                    creg_tok = {}
                    ppt_free = {}
                    ptb_free = {}

                    def emit_scores(i):
                        gp_, r, h4 = RH[i]
                        h = gp_ * 4 + h4
                        nk = nkb_of(grp[r]) * 128
                        hp = i % 2
                        toks = []
                        for c0 in range(0, nk, 1024):
                            w = min(1024, nk - c0)
                            par = state["creg_i"] % 2
                            state["creg_i"] += 1
                            reg = psc[:, par * 1024:par * 1024 + 1024]
                            t_mm = None
                            for m0 in range(0, w, 512):
                                mw = min(512, w - m0)
                                wts = []
                                if m0 == 0:
                                    wts = [kv_tok[gp_][0], t_qload, creg_tok.get(par)]
                                t_mm = p.on("tensor", (lambda reg, m0, mw, h, r, c0, gp_: lambda e: e.matmul(
                                    reg[:, m0:m0 + mw],
                                    qg[:, h * 512 + r * 128: h * 512 + (r + 1) * 128],
                                    kbuf[gp_ % 2][:, c0 + m0: c0 + m0 + mw],
                                    start=True, stop=True))(reg, m0, mw, h, r, c0, gp_),
                                    waits=wts, inc="pe" if m0 + 512 >= w else None)
                            t_ev = p.on("scalar", (lambda hp, c0, w, reg: lambda e: e.copy(
                                sc32[hp][:, c0:c0 + w], reg[:, 0:w]))(hp, c0, w, reg),
                                waits=[t_mm, sm_done.get((i - 2, "max"))], inc="act")
                            creg_tok[par] = t_ev
                            toks.append(t_ev)
                        st_done[i] = toks

                    def emit_softmax(i):
                        gp_, r, h4 = RH[i]
                        nk = nkb_of(grp[r]) * 128
                        hp = i % 2
                        t_mk = p.on("vector", (lambda hp, nk, r: lambda e: e.tensor_add(
                            sc32[hp][:, nk - 256:nk], sc32[hp][:, nk - 256:nk],
                            mB[:, r * 256:(r + 1) * 256]))(hp, nk, r),
                            waits=st_done[i] + [t_m], inc="dve")
                        t_mx = p.on("vector", (lambda hp, nk: lambda e: e.reduce_max(
                            negmax[hp][:], sc32[hp][:, 0:nk], axis=AX.X, negate=True))(hp, nk),
                            waits=[t_mk], inc="dve")
                        sm_done[(i, "max")] = t_mx
                        t_xp = p.on("scalar", (lambda hp, nk: lambda e: e.activation(
                            pbuf[hp][:, 0:nk], sc32[hp][:, 0:nk], AF.Exp,
                            bias=negmax[hp][:], scale=1.0,
                            accum_out=ssum[hp][:]))(hp, nk),
                            waits=[t_mx, ctxmm.get((i - 2, "alltr"))], inc="act")
                        sm_done[(i, "exp")] = t_xp
                        t_rc = p.on("vector", (lambda hp: lambda e: e.reciprocal(
                            rsum[hp][:], ssum[hp][:]))(hp), waits=[t_xp], inc="dve")
                        t_sc = p.on("vector", (lambda hp, nk: lambda e: e.scalar_tensor_tensor(
                            pbuf[hp][:, 0:nk], pbuf[hp][:, 0:nk], rsum[hp][:],
                            pbuf[hp][:, 0:nk], op0=ALU.mult, op1=ALU.bypass))(hp, nk),
                            waits=[t_rc], inc="dve")
                        sm_done[(i, "scale")] = t_sc

                    def emit_ctx(i):
                        gp_, r, h4 = RH[i]
                        h = gp_ * 4 + h4
                        nkb = nkb_of(grp[r])
                        hp = i % 2
                        cp = i % 2
                        creg = pctx[:, cp * 512: cp * 512 + 128]
                        nbb = (nkb + 7) // 8
                        t_cm = None
                        for bb in range(nbb):
                            u0, u1 = bb * 8, min(nkb, bb * 8 + 8)
                            tpar = state["ptreg_i"] % 2
                            state["ptreg_i"] += 1
                            regt = ppt[:, tpar * 1024:(tpar + 1) * 1024]
                            t_tr = None
                            for u in range(u0, u1):
                                wts = []
                                if u == u0:
                                    wts = [sm_done[(i, "scale")], ppt_free.get(tpar)]
                                t_tr = p.on("tensor", (lambda regt, u, u0, hp: lambda e: e.transpose(
                                    regt[:, (u - u0) * 128:(u - u0 + 1) * 128],
                                    pbuf[hp][:, u * 128:(u + 1) * 128], c_ident[:]))(regt, u, u0, hp),
                                    waits=wts, inc="pe" if u == u1 - 1 else None)
                            t_pe = p.on("vector", (lambda tpar, regt, u0, u1: lambda e: e.tensor_copy(
                                ptb[tpar][:, 0:(u1 - u0) * 128], regt[:, 0:(u1 - u0) * 128]))(tpar, regt, u0, u1),
                                waits=[t_tr, ptb_free.get(tpar)], inc="dve")
                            ppt_free[tpar] = t_pe
                            for u in range(u0, u1):
                                wts = [t_pe]
                                if u == 0:
                                    wts += [ctx_done.get(i - 2), kv_tok[gp_][1]]
                                t_cm = p.on("tensor", (lambda creg, u, u0, tpar, gp_, nkb: lambda e: e.matmul(
                                    creg, vbuf[gp_ % 2][:, u * 128:(u + 1) * 128],
                                    ptb[tpar][:, (u - u0) * 128:(u - u0 + 1) * 128],
                                    start=(u == 0), stop=(u == nkb - 1)))(creg, u, u0, tpar, gp_, nkb),
                                    waits=wts, inc="pe" if u == u1 - 1 else None)
                            ptb_free[tpar] = t_cm
                        ctxmm[(i, "alltr")] = t_cm
                        t_ce = p.on("scalar", (lambda h, r, creg: lambda e: e.copy(
                            actT[:, h * 512 + r * 128: h * 512 + (r + 1) * 128], creg))(h, r, creg),
                            waits=[t_cm], inc="act")
                        ctx_done[i] = t_ce

                    PIPELINED_B2 = True
                    for i, (gp_, r, h4) in enumerate(RH):
                        if r == 0 and h4 == 1 and gp_ + 1 < 8:
                            wts = [p.at("pe")]
                            p.dma("sync", kbuf[(gp_ + 1) % 2][:, 0:nkbm * 128],
                                  k_sc[gp_ + 1][:, 0:nkbm * 128], waits=wts)
                            p.dma("sync", vbuf[(gp_ + 1) % 2][:, 0:nkbm * 128],
                                  v_sc[gp_ + 1][:, 0:nkbm * 128], waits=wts)
                            t_ekv = dma_echo("sync")
                            kv_tok[gp_ + 1] = [t_ekv, t_ekv]
                        if PIPELINED_B2:
                            emit_scores(i)
                            if i >= 1:
                                emit_softmax(i - 1)
                                emit_ctx(i - 1)
                        else:
                            emit_scores(i)
                            emit_softmax(i)
                            emit_ctx(i)
                    if PIPELINED_B2:
                        emit_softmax(len(RH) - 1)
                        emit_ctx(len(RH) - 1)
                    t_attn_done = p.at("act")
                phase_barrier()

                with ExitStack() as c3:
                    pw = c3.enter_context(nc.psum_tensor(f"pB_wo{g}", [128, 4096], F32))
                    wo_tok = {}
                    mm_done = {}
                    ev_done = {}
                    p.dma(
                        "gpsimd", wpanB[0][:].rearrange("p (c o) -> p c o", c=NIC),
                        wo[:, 0:256].rearrange("(c p) o -> p c o", p=128),
                        waits=[t_b5_pe.get(g - 1)])
                    wo_tok[0] = dma_echo("gpsimd")
                    for op in range(16):
                        if op + 1 < 16:
                            p.dma(
                                "gpsimd", wpanB[(op + 1) % 2][:].rearrange("p (c o) -> p c o", c=NIC),
                                wo[:, (op + 1) * 256:(op + 2) * 256].rearrange("(c p) o -> p c o", p=128),
                                waits=[mm_done.get(op - 1)])
                            wo_tok[op + 1] = dma_echo("gpsimd")
                        wp = wpanB[op % 2]
                        for t in range(4):
                            bk_i = (op * 4 + t) % 8
                            bank = pw[:, bk_i * 512: bk_i * 512 + 256]
                            for ic in range(NIC):
                                wts = []
                                if ic == 0:
                                    wts = [wo_tok[op], t_attn_done, ev_done.get((op - 2, t))]
                                p.on("tensor", (lambda bank, ic, t, wp: lambda e: e.matmul(
                                    bank, actT[:, ic * 512 + t * 128: ic * 512 + (t + 1) * 128],
                                    wp[:, ic * 256:(ic + 1) * 256],
                                    start=(ic == 0), stop=False))(bank, ic, t, wp),
                                    waits=wts, inc=None)
                            t_mm = p.on("tensor", (lambda bank, op: lambda e: e.matmul(
                                bank, c_ones[:], c_bo[:, op * 256:(op + 1) * 256],
                                start=False, stop=True))(bank, op),
                                waits=[t_bo] if (op == 0 and t == 0) else [], inc="pe")
                            mm_done[op] = t_mm
                            apar = t % 2
                            t_ev = p.on("scalar", (lambda apar, bank: lambda e: e.copy(
                                att_st[apar][:, 0:256], bank))(apar, bank),
                                waits=[t_mm, ev_done.get((op, t - 2)),
                                       ev_done.get((op - 1, t + 2))], inc="act")
                            ev_done[(op, t)] = p.dma(
                                "sync", x1_sc[t][:, op * 256:(op + 1) * 256],
                                att_st[apar][:, 0:256], waits=[t_ev])
                    t_b3_done = p.at("sp")
                phase_barrier()

                with ExitStack() as c4:
                    ptr4 = c4.enter_context(nc.psum_tensor(f"pB_tr4{g}", [128, 2048], BF16))
                    tr_done4 = {}
                    add_done = {}
                    x1_spill = {}
                    for t in range(4):
                        t_xa = p.dma("sync", attnB[:], x1_sc[t],
                                     waits=[t_b3_done, add_done.get(t - 1),
                                            x1_spill.get(t - 1)])
                        t_xo = p.dma("sync", xB[:], x_own[g * 4 + t],
                                     waits=[add_done.get(t - 1), t_attn_done,
                                            sm_done.get((len(RH) - 1, "scale"))])
                        t_exo = dma_echo("sync")
                        t_add = p.on("vector", lambda e: e.tensor_add(attnB[:], attnB[:], xB[:]),
                                     waits=[t_xa, t_xo, t_exo], inc="dve")
                        x1_spill[t] = p.dma("sync", x1_sc[t], attnB[:], waits=[t_add])
                        t_ln = emit_ln(attnB, xsqB, lnbB, statB, t_add,
                                       extra_tc=[tr_done4.get((t - 1, 3))])
                        add_done[t] = t_ln
                        for bb in range(4):
                            tpar = (t * 4 + bb) % 2
                            reg = ptr4[:, tpar * 1024:(tpar + 1) * 1024]
                            war = tr_done4.get((t, bb - 2)) if bb >= 2 else tr_done4.get((t - 1, bb + 2))
                            t_tr = None
                            for u in range(8):
                                icx = bb * 8 + u
                                t_tr = p.on("tensor", (lambda reg, u, icx: lambda e: e.transpose(
                                    reg[:, u * 128:(u + 1) * 128],
                                    lnbB[:, icx * 128:(icx + 1) * 128], c_ident[:]))(reg, u, icx),
                                    waits=[t_ln, war] if u == 0 else [],
                                    inc="pe" if u == 7 else None)
                            dst = actT_v[:, bb * 8:(bb + 1) * 8, t * 128:(t + 1) * 128]
                            tr_done4[(t, bb)] = p.on(
                                "scalar", (lambda dst, reg: lambda e: e.copy(
                                    dst, reg[:].rearrange("p (c u) -> p c u", c=8)))(dst, reg),
                                waits=[t_tr], inc="act")
                    t_b4_done = p.at("act")
                phase_barrier()

                with ExitStack() as c5:
                    pw = c5.enter_context(nc.psum_tensor(f"pB_ffn{g}", [128, 4096], F32))
                    wt_tok = {}
                    mm_done5 = {}
                    ev_done = {}
                    p.dma("sync", wpanB[0][:].rearrange("p (c o) -> p c o", c=NIC),
                          wt_sc[:, :, 0:256].rearrange("c p o -> p c o"),
                          waits=[t_p0_done, t_b4_done])
                    wt_tok[0] = dma_echo("sync")
                    for op in range(16):
                        if op + 1 < 16:
                            p.dma(
                                "sync", wpanB[(op + 1) % 2][:].rearrange("p (c o) -> p c o", c=NIC),
                                wt_sc[:, :, (op + 1) * 256:(op + 2) * 256].rearrange("c p o -> p c o"),
                                waits=[mm_done5.get(op - 1)])
                            wt_tok[op + 1] = dma_echo("sync")
                        wp = wpanB[op % 2]
                        for t in range(4):
                            xpar = (op * 4 + t) % 2
                            p.dma("sync", x1sl[xpar][:, 0:256],
                                  x1_sc[t][:, op * 256:(op + 1) * 256],
                                  waits=[x1_spill.get(t),
                                         ev_done.get((op, t - 2)),
                                         ev_done.get((op - 1, t + 2))])
                            t_x1 = dma_echo("sync")
                            bk_i = (op * 4 + t) % 8
                            bank = pw[:, bk_i * 512: bk_i * 512 + 256]
                            t_mm = None
                            for ic in range(NIC):
                                wts = []
                                if ic == 0:
                                    wts = [wt_tok[op], t_b4_done, ev_done.get((op - 2, t))]
                                t_mm = p.on("tensor", (lambda bank, ic, t, wp: lambda e: e.matmul(
                                    bank, actT[:, ic * 512 + t * 128: ic * 512 + (t + 1) * 128],
                                    wp[:, ic * 256:(ic + 1) * 256],
                                    start=(ic == 0), stop=(ic == NIC - 1)))(bank, ic, t, wp),
                                    waits=wts, inc="pe" if ic == NIC - 1 else None)
                            mm_done5[op] = t_mm
                            ypar = (op * 4 + t) % 2
                            t_ev = p.on("vector", (lambda ypar, xpar, bank: lambda e: e.tensor_add(
                                yst[ypar][:, 0:256], bank, x1sl[xpar][:, 0:256]))(ypar, xpar, bank),
                                waits=[t_mm, t_x1, ev_done.get((op, t - 2))], inc="dve")
                            ev_done[(op, t)] = p.dma(
                                "sync", y_out[g * 4 + t][:, op * 256:(op + 1) * 256],
                                yst[ypar][:, 0:256], waits=[t_ev])
                    t_b5_sp[g] = p.at("sp")
                    t_b5_pe[g] = p.at("pe")

        p.on("vector", lambda e: e.engine_nop(),
             waits=[p.at("sp"), p.at("gp")], inc=None)

        with nc.Block() as block:
            p.build(block, ctx)
    return nc


def _host_tables():
    half = 64
    inv_freq = (1.0 / (10000.0 ** (np.arange(0, half, dtype=np.float32) / half))
                ).astype(np.float32)
    pos = np.arange(S, dtype=np.float32)
    ang = pos[None, :] * inv_freq[:, None]
    cos = np.cos(ang).astype(np.float32)
    sin = np.sin(ang).astype(np.float32)
    ccg = np.concatenate([cos, cos], axis=0)
    ssg = np.concatenate([-sin, sin], axis=0)
    return ccg, ssg


def kernel(L0_attn_hidden_states, L0_attn_Wq, L0_attn_bq, L0_attn_Wk, L0_attn_bk,
           L0_attn_Wv, L0_attn_bv, L0_attn_Wo, L0_attn_bo, L0_ffn_bases, L0_ffn_alphas):
    x = np.ascontiguousarray(np.asarray(L0_attn_hidden_states, dtype=np.float32))
    wq_h = np.ascontiguousarray(np.asarray(L0_attn_Wq, dtype=np.float32))
    wk_h = np.ascontiguousarray(np.asarray(L0_attn_Wk, dtype=np.float32))
    wv_h = np.ascontiguousarray(np.asarray(L0_attn_Wv, dtype=np.float32))
    wo_h = np.ascontiguousarray(np.asarray(L0_attn_Wo, dtype=np.float32))
    bq = np.asarray(L0_attn_bq, dtype=np.float32)
    bk = np.asarray(L0_attn_bk, dtype=np.float32)
    bv = np.asarray(L0_attn_bv, dtype=np.float32)
    bo = np.asarray(L0_attn_bo, dtype=np.float32)
    bases = np.asarray(L0_ffn_bases)
    alphas = np.asarray(L0_ffn_alphas, dtype=np.float32)

    ccg, ssg = _host_tables()
    bases_t = np.ascontiguousarray(bases.transpose(0, 2, 1))
    alpha_row = np.ascontiguousarray(alphas.reshape(1, 4 * DM))
    bq_t = np.ascontiguousarray(bq.reshape(32, 128).T)
    bk_t = np.ascontiguousarray(bk.reshape(8, 128).T)
    bv_t = np.ascontiguousarray(bv.reshape(8, 128).T)
    bo_row = np.ascontiguousarray(bo.reshape(1, DM))
    ones_r = np.ones((1, 128), dtype=np.float32)
    identm = np.eye(128, dtype=np.float32)
    swapm = np.zeros((128, 128), dtype=np.float32)
    for i in range(128):
        swapm[(i + 64) % 128, i] = 1.0

    tri = np.where(np.arange(128)[None, :] <= np.arange(128)[:, None],
                   np.float32(0.0), np.float32(NEGBIG)).astype(np.float32)
    zeros = np.zeros((128, 128), dtype=np.float32)
    negall = np.full((128, 128), NEGBIG, dtype=np.float32)

    in_maps = []
    for core in range(8):
        b, par = core // 2, core % 2
        tiles = own_tiles(par)
        xb = x[b]
        x_own_h = np.ascontiguousarray(xb.reshape(NT, 128, DM)[tiles])
        ccg4 = ccg.reshape(128, NT, 128)
        cco = np.ascontiguousarray(ccg4[:, tiles, :].reshape(128, 2048))
        ssg4 = ssg.reshape(128, NT, 128)
        sso = np.ascontiguousarray(ssg4[:, tiles, :].reshape(128, 2048))
        mm = np.empty((16, 128, 256), dtype=np.float32)
        for r in range(16):
            gtile = tiles[r]
            nkb = nkb_of(r)
            if gtile == nkb - 1:
                mm[r, :, 0:128] = zeros
                mm[r, :, 128:256] = tri
            else:
                assert gtile == nkb - 2, (r, gtile, nkb)
                mm[r, :, 0:128] = tri
                mm[r, :, 128:256] = negall
        in_maps.append(dict(
            x_full=xb, x_own=x_own_h, wq=wq_h, wk=wk_h, wv=wv_h, wo=wo_h,
            bq_t=bq_t, bk_t=bk_t, bv_t=bv_t, bo_row=bo_row,
            bases_t=bases_t, alpha_row=alpha_row,
            ccg=ccg, ssg=ssg, cco=cco, sso=sso, m_mask=mm,
            ones_r=ones_r, ident=identm, swapm=swapm,
        ))

    nc = build_nc()
    res = run_bass_kernel_spmd(nc, in_maps, core_ids=list(range(8)))

    out = np.empty((B, S, DM), dtype=np.float32)
    for core in range(8):
        b, par = core // 2, core % 2
        tiles = own_tiles(par)
        yc = np.asarray(res.results[core]["y"])
        ob = out[b].reshape(NT, 128, DM)
        for r, gtile in enumerate(tiles):
            ob[gtile] = yc[r]
    return out
